# version 31
# baseline (speedup 1.0000x reference)
import numpy as np
import ml_dtypes
from types import SimpleNamespace
from contextlib import ExitStack

import concourse.bass as bass
import concourse.mybir as mybir
from concourse.bacc import Bacc
from concourse.bass_utils import run_bass_kernel_spmd
import concourse.dve_ops as dve_ops
from concourse.dve_op import DveOp
from concourse.dve_spec import (Spec, Src0, Src1, C0, C1, C2, One, select, sq,
                                lower)

F32 = mybir.dt.float32
BF16 = mybir.dt.bfloat16
I16 = mybir.dt.int16
Act = mybir.ActivationFunctionType
Alu = mybir.AluOpType
AxX = mybir.AxisListType.X

B, L, DM = 4, 4096, 1024
NCORES = 8
TOK = 2048
NT = TOK // 128
NDFT = 128
NF = NDFT // 2
STRIDE = 64
EPS = 1e-5
BF = ml_dtypes.bfloat16

_q2 = sq(Src0)
_QZ_SPEC = Spec(body=Src0 * select(_q2 < C0, One, select(_q2 < C2, C1, C1 * C1)))
_QN_SPEC = Spec(body=Src1 * select(_q2 < C0, One, select(_q2 < C2, C1, C1 * C1)))


def _qz_ref(in0, in1, s0, s1, imm2):
    t = np.asarray(in0)
    q2 = t * t
    s = np.where(q2 < s0, 1.0, np.where(q2 < imm2, s1, s1 * s1))
    return np.rint(t * s)


def _qn_ref(in0, in1, s0, s1, imm2):
    t = np.asarray(in0)
    q2 = t * t
    iv = np.where(q2 < s0, 1.0, np.where(q2 < imm2, s1, s1 * s1))
    return np.asarray(in1) * iv


def _register_op(name, spec, reference):
    for o in dve_ops.OPS:
        if o.name == name:
            return o
    compiled = lower(spec, ver="v3", perf_en=False, name=name)
    op = DveOp(name, spec, subdim=False, uops_sha={"v3": compiled.sha("v3")},
               reference=reference)
    dve_ops.OPS.append(op)
    row = dve_ops._CUSTOM_DVE_ROW_BASE + len(dve_ops.OPS) - 1
    assert row < 0x20, "custom DVE row budget exceeded"
    dve_ops._SUB_OPCODE_FOR_NAME[name] = row
    dve_ops.CUSTOM_DVE_SPECS[name] = spec
    dve_ops.CUSTOM_DVE_REFS[name] = reference
    return op


QZ_OP = _register_op("MK_QZ", _QZ_SPEC, _qz_ref)
QN_OP = _register_op("MK_QN", _QN_SPEC, _qn_ref)


def _host_consts():
    f = np.arange(NF, dtype=np.float64)
    j = np.arange(NDFT, dtype=np.float64)
    th = 2.0 * np.pi * (f + 0.5) / NDFT
    ang = np.outer(j, th)
    fstack = np.concatenate([np.cos(ang), -np.sin(ang)], axis=1)
    jo = np.arange(STRIDE, NDFT, dtype=np.float64)
    angi = np.outer(th, jo)
    wc = (2.0 / NDFT) * np.cos(angi)
    ws = -(2.0 / NDFT) * np.sin(angi)
    w1 = np.concatenate([wc, -wc], axis=0)
    w2 = np.concatenate([ws, ws], axis=0)
    z64 = np.zeros((64, 128), np.float64)
    fsa_lo = np.concatenate([z64, fstack[0:64]], axis=0)
    fsa_hi = np.concatenate([fstack[64:128], z64], axis=0)
    zpad = np.zeros((64, 640 - 512 - 128), np.float64)
    w12 = np.concatenate([np.concatenate([w1, w2], axis=1)], axis=1)
    pack = np.concatenate([fstack, fsa_lo, fsa_hi, np.eye(128), w12], axis=1)
    ths = np.stack([np.cos(th), np.sin(th)], axis=1)
    return {
        "c_pack": np.ascontiguousarray(pack.astype(BF)),
        "c_ths": np.ascontiguousarray(ths.astype(np.float32)),
    }


class Plan:
    def __init__(self):
        self.streams = {}
        self.vals = {}
        self.all = []

    def add(self, engine, fn, deps=(), inc=1, tag=""):
        val = self.vals.get(engine, 0) + inc
        self.vals[engine] = val
        o = SimpleNamespace(engine=engine, fn=fn,
                            deps=[d for d in deps if d is not None],
                            inc=inc, val=val, tag=tag)
        self.streams.setdefault(engine, []).append(o)
        self.all.append(o)
        return o

    def dma(self, engine, fn, deps=(), tag=""):
        return self.add(engine, fn, deps, inc=16, tag=tag)


def build_nc():
    nc = Bacc("TRN2", target_bir_lowering=False)
    g = SimpleNamespace()

    d_u = nc.dram_tensor("u", [TOK, DM], F32, kind="ExternalInput").ap()
    d_uh = nc.dram_tensor("uhalo", [128, DM], F32, kind="ExternalInput").ap()
    d_par = nc.dram_tensor("par", [8, DM], F32, kind="ExternalInput").ap()
    d_crr = nc.dram_tensor("crr", [NF, DM], F32, kind="ExternalInput").ap()
    d_cir = nc.dram_tensor("cir", [NF, DM], F32, kind="ExternalInput").ap()
    d_brr = nc.dram_tensor("brr", [NF, DM], F32, kind="ExternalInput").ap()
    d_bir = nc.dram_tensor("bir", [NF, DM], F32, kind="ExternalInput").ap()
    d_arr = nc.dram_tensor("arr", [NF, DM], F32, kind="ExternalInput").ap()
    d_air = nc.dram_tensor("air", [NF, DM], F32, kind="ExternalInput").ap()
    d_drep = nc.dram_tensor("drep", [128, DM], F32, kind="ExternalInput").ap()
    d_w = nc.dram_tensor("w", [DM, DM], F32, kind="ExternalInput").ap()
    d_pack = nc.dram_tensor("c_pack", [128, 640], BF16, kind="ExternalInput").ap()
    d_ths = nc.dram_tensor("c_ths", [NF, 2], F32, kind="ExternalInput").ap()
    d_out = nc.dram_tensor("out", [TOK, DM], F32, kind="ExternalOutput").ap()

    es = ExitStack()

    def sb(name, shape, dt):
        t = es.enter_context(nc.sbuf_tensor(name, shape, dt))
        setattr(g, name, t)
        return t

    def psum(name, shape, dt=F32):
        t = es.enter_context(nc.psum_tensor(name, shape, dt))
        setattr(g, name, t)
        return t

    sb("pBias", [1, DM], F32)
    sb("packc", [128, 640], BF16)
    sb("thsc", [NF, 2], F32)
    g.fstack = g.packc[:, 0:128]
    g.fsalo = g.packc[:, 128:256]
    g.fsahi = g.packc[:, 256:384]
    g.identb = g.packc[:, 384:512]
    g.w1m = g.packc[:, 512:576]
    g.w2m = g.packc[:, 576:640]
    g.cth = g.thsc[:, 0:1]
    g.sth = g.thsc[:, 1:2]
    sb("ones1", [1, 128], BF16)
    sb("biasb", [1, DM], BF16)
    sb("d1rep", [128, DM], F32)
    sb("ks1", [128, DM], BF16)
    sb("ks2", [128, DM], BF16)
    sb("w_bmax", [128, 128], F32)
    sb("w_sb", [128, 64], BF16)
    sb("wdT", [128, 8 * DM], BF16)
    sb("u_t", [128, 2 * DM], F32)
    sb("ubfB", [128, 3 * DM], BF16)
    sb("uevA", [128, DM], BF16)
    sb("uevB", [128, DM], BF16)
    sb("p1A", [128, DM], BF16)
    sb("p2A", [128, DM], BF16)
    sb("p1B", [128, DM], BF16)
    sb("p2B", [128, DM], BF16)
    sb("tD", [128, 2 * DM], F32)
    sb("resid", [128, 2 * DM], F32)
    sb("rsum", [128, 2], F32)
    sb("rsq", [128, 2], F32)
    sb("sqjunk", [128, DM], F32)
    sb("st_m", [128, 2], F32)
    sb("st_r", [128, 2], F32)
    sb("st_t1", [128, 12], F32)
    sb("st_t2", [128, 12], F32)
    sb("st2_m", [128, 2], F32)
    sb("st2_r", [128, 2], F32)
    sb("rm", [128, DM], F32)
    sb("bmax", [128, 32], F32)
    sb("binv", [128, 32], F32)
    sb("r8B", [128, 32], F32)
    sb("rr8B", [128, 64], F32)
    sb("sbB", [128, 64], BF16)
    sb("tq", [128, DM], F32)
    sb("qi", [128, DM], I16)
    sb("iv", [128, DM], BF16)
    sb("qn", [128, DM], BF16)
    sb("xd", [128, 2 * DM], BF16)
    sb("x1b", [128, 2 * DM], BF16)
    sb("xdT", [128, 2 * DM], BF16)
    sb("z", [128, 2 * DM], F32)
    sb("zsum", [128, 2], F32)
    sb("zsq", [128, 2], F32)
    sb("obuf", [128, 2 * DM], F32)

    sb("kbuf", [NF, 8 * DM], BF16)
    sb("arep", [NF, DM], BF16)
    sb("airep", [NF, DM], BF16)
    g.grrep = g.x1b[0:NF, 0:DM]
    g.girep = g.x1b[0:NF, DM:2 * DM]
    g.kf1 = g.z[0:NF, 0:DM]
    g.kf2 = g.z[0:NF, DM:2 * DM]
    g.kinv1 = g.resid[0:NF, 0:DM]
    g.kinv2 = g.resid[0:NF, DM:2 * DM]
    g.wstage = g.obuf
    g.w_qi = g.qi[:]
    g.w_qn = g.qn[:]
    g.w_d = g.xd[:, DM:2 * DM]
    g.kb = [g.kbuf[:, i * DM:(i + 1) * DM] for i in range(8)]
    g.stg_cr = g.z[0:NF, 0:DM]
    g.stg_br = g.z[0:NF, DM:2 * DM]
    g.stg_ci = g.resid[0:NF, 0:DM]
    g.stg_bi = g.resid[0:NF, DM:2 * DM]
    g.stg_ar = g.tD[0:NF, DM:2 * DM]
    g.stg_ai = g.d1rep[0:NF, 0:DM]

    psum("psF", [128, DM])
    psum("psY", [128, DM])
    psum("psT", [128, 2 * DM], BF16)
    psum("psZ", [128, DM])

    sems = {nm: es.enter_context(nc.semaphore("sem_" + nm))
            for nm in ["sync", "gd", "pe", "v", "s", "gc", "sd"]}

    P = Plan()
    V = lambda fn, deps=(), tag="": P.add("v", fn, deps, tag=tag)
    S = lambda fn, deps=(), tag="": P.add("s", fn, deps, tag=tag)
    G = lambda fn, deps=(), tag="": P.add("gc", fn, deps, tag=tag)
    PE = lambda fn, deps=(), tag="": P.add("pe", fn, deps, tag=tag)
    GD = lambda fn, deps=(), tag="": P.dma("gd", fn, deps, tag=tag)
    SY = lambda fn, deps=(), tag="": P.dma("sync", fn, deps, tag=tag)
    SD = lambda fn, deps=(), tag="": P.dma("sd", fn, deps, tag=tag)

    ld_pack = SY(lambda e: e.dma_start(out=g.packc[:], in_=d_pack))
    ld_ths = SY(lambda e: e.dma_start(out=g.thsc[:], in_=d_ths))
    ld_par = SY(lambda e: e.dma_start(out=g.pBias[:], in_=d_par[7:8, :]))
    ld_id = ld_fs = ld_fsalo = ld_fsahi = ld_w1 = ld_w2 = ld_pack
    ld_cth = ld_sth = ld_ths

    ld_u = [None] * NT

    def em_ldu(i):
        return lambda e: e.dma_start(out=g.u_t[:, (i % 2) * DM:(i % 2 + 1) * DM],
                                     in_=d_u[i * 128:(i + 1) * 128, :])

    ld_u[0] = SD(em_ldu(0), tag="u0")
    ld_uh = SD(lambda e: e.dma_start(out=g.u_t[:, DM:2 * DM], in_=d_uh))
    ld_crr = SD(lambda e: e.dma_start(out=g.stg_cr, in_=d_crr))
    ld_brr = SD(lambda e: e.dma_start(out=g.stg_br, in_=d_brr))
    ld_cir = SD(lambda e: e.dma_start(out=g.stg_ci, in_=d_cir))
    ld_bir = SD(lambda e: e.dma_start(out=g.stg_bi, in_=d_bir))
    ld_arr = SD(lambda e: e.dma_start(out=g.stg_ar, in_=d_arr))
    ld_air = SD(lambda e: e.dma_start(out=g.stg_ai, in_=d_air))

    wslot = [g.wstage[:, 0:DM], g.wstage[:, DM:2 * DM], g.sqjunk[:, 0:DM]]

    def em_ldw_lo(r):
        return lambda e: e.dma_start(out=wslot[r % 3][0:64, :],
                                     in_=d_w[r * 128:r * 128 + 64, :])

    def em_ldw_hi(r):
        return lambda e: e.dma_start(out=wslot[r % 3][64:128, :],
                                     in_=d_w[r * 128 + 64:(r + 1) * 128, :])

    w_lo = [None] * 8
    w_hi = [None] * 8
    for _r in range(3):
        w_lo[_r] = GD(em_ldw_lo(_r), tag=f"wlo{_r}")
        w_hi[_r] = SY(em_ldw_hi(_r), tag=f"whi{_r}")
    ld_drep = GD(lambda e: e.dma_start(out=g.tD[:, 0:DM], in_=d_drep))

    ones_set = V(lambda e: e.memset(g.ones1[:], 1.0))

    w_q2c = [None] * 8
    w_q2sb = [None] * 8
    w_tw = [None] * 8
    w_qz = [None] * 8
    w_qd = [None] * 8
    w_q7 = [None] * 8
    w_trs = [None] * 8
    w_ev = [None] * 8

    def w_head(r):
        wst = wslot[r % 3]
        bsl = slice((r % 2) * 32, (r % 2 + 1) * 32)
        rsl = slice(64 + (r % 2) * 32, 64 + (r % 2 + 1) * 32)
        q1 = V(lambda e, wst=wst, bsl=bsl: e.tensor_reduce(
            g.w_bmax[:, bsl], wst.rearrange("p (a b) -> p a b", a=32, b=32),
            AxX, Alu.max, apply_absolute_value=True),
            [w_lo[r], w_hi[r], w_tw[r - 3] if r >= 3 else None], tag=f"wq1_{r}")
        q2a = V(lambda e, bsl=bsl: e.tensor_scalar(g.w_bmax[:, bsl], g.w_bmax[:, bsl], 7.5e-30, None, Alu.max), [q1])
        q2b = V(lambda e, bsl=bsl, rsl=rsl: e.reciprocal(g.w_bmax[:, rsl], g.w_bmax[:, bsl]),
                [q2a, w_tw[r - 2] if r >= 2 else None])
        w_q2c[r] = V(lambda e, rsl=rsl: e.tensor_scalar(g.w_bmax[:, rsl], g.w_bmax[:, rsl], 60.0, None, Alu.mult), [q2b])
        w_q2sb[r] = V(lambda e, bsl=bsl, ssl=slice((r % 2) * 32, (r % 2 + 1) * 32): e.tensor_scalar(
            g.w_sb[:, ssl], g.w_bmax[:, bsl], 1.0 / 60.0, None, Alu.mult),
            [q2a, w_q7[r - 2] if r >= 2 else None], tag=f"wsb_{r}")
        tbuf = g.rm[:] if r % 2 == 0 else g.tq[:]
        r8v = g.w_bmax[:, rsl].unsqueeze(2).broadcast_to([128, 32, 32])
        w_tw[r] = G(lambda e, wst=wst, r8v=r8v, tbuf=tbuf: e.tensor_tensor(
            tbuf.rearrange("p (a b) -> p a b", a=32, b=32),
            wst.rearrange("p (a b) -> p a b", a=32, b=32), r8v, Alu.mult),
            [w_q2c[r], w_qd[r - 2] if r >= 2 else None], tag=f"wtw_{r}")
        if r + 3 < 8:
            w_lo[r + 3] = GD(em_ldw_lo(r + 3), [w_tw[r]], tag=f"wlo{r+3}")
            w_hi[r + 3] = SY(em_ldw_hi(r + 3), [w_tw[r]], tag=f"whi{r+3}")

    def w_tail(r):
        tbuf = g.rm[:] if r % 2 == 0 else g.tq[:]
        ssl = slice((r % 2) * 32, (r % 2 + 1) * 32)
        w_qz[r] = V(lambda e, tbuf=tbuf: e._custom_dve(
            QZ_OP, out=g.w_qi, in0=tbuf, s0=256.0, s1=0.5, imm2=1024.0),
            [w_tw[r], w_qd[r - 1] if r >= 1 else None], tag=f"wqz_{r}")
        w_qd[r] = V(lambda e, tbuf=tbuf: e._custom_dve(
            QN_OP, out=g.w_qn, in0=tbuf, in1=g.w_qi, s0=256.0, s1=2.0, imm2=1024.0),
            [w_qz[r], w_q7[r - 1] if r >= 1 else None], tag=f"wqd_{r}")
        sbv = g.w_sb[:, ssl].unsqueeze(2).broadcast_to([128, 32, 32])
        w_q7[r] = G(lambda e, sbv=sbv: e.tensor_tensor(
            g.w_d[:, 0:DM].rearrange("p (a b) -> p a b", a=32, b=32),
            g.w_qn[:].rearrange("p (a b) -> p a b", a=32, b=32), sbv, Alu.mult),
            [w_qd[r], w_q2sb[r], w_trs[r - 1] if r >= 1 else None], tag=f"wq7_{r}")
        trs = []
        psl = (r % 2) * DM
        for c in range(8):
            d0 = [w_q7[r], ld_id, w_ev[r - 2] if r >= 2 else None] if c == 0 else [w_q7[r]]
            trs.append(PE(lambda e, c=c, psl=psl: e.transpose(
                g.psT[:, psl + c * 128:psl + (c + 1) * 128],
                g.w_d[:, c * 128:(c + 1) * 128],
                g.identb), d0, tag=f"wtr_{r}_{c}"))
        w_trs[r] = trs[-1]
        wdT_dst = g.wdT[:].rearrange("p (c n) -> p c n", c=8, n=DM)[:, :, r * 128:(r + 1) * 128]
        w_ev[r] = S(lambda e, wdT_dst=wdT_dst, psl=psl: e.activation(
            wdT_dst, g.psT[:, psl:psl + DM].rearrange("p (a b) -> p a b", a=8, b=128),
            Act.Copy),
            [w_trs[r], w_ev[r - 1] if r >= 1 else None], tag=f"wev_{r}")

    for r in range(9):
        if r < 8:
            w_head(r)
        if r >= 1:
            w_tail(r - 1)
    w_done = w_ev[7]

    gp1 = V(lambda e: e.tensor_tensor(g.kb[0], g.stg_cr, g.stg_br, Alu.mult), [ld_crr, ld_brr], tag="gp1")
    gp2 = V(lambda e: e.tensor_tensor(g.kb[1], g.stg_ci, g.stg_bi, Alu.mult), [ld_cir, ld_bir], tag="gp2")
    gp3 = V(lambda e: e.tensor_tensor(g.grrep, g.kb[0], g.kb[1], Alu.subtract), [gp1, gp2], tag="gp3")
    gp4 = V(lambda e: e.tensor_tensor(g.kb[0], g.stg_cr, g.stg_bi, Alu.mult), [gp3], tag="gp4")
    gp5 = V(lambda e: e.tensor_tensor(g.kb[1], g.stg_ci, g.stg_br, Alu.mult), [gp4], tag="gp5")
    rep_gi = V(lambda e: e.tensor_tensor(g.girep, g.kb[0], g.kb[1], Alu.add), [gp5], tag="gp6")
    rep_gr = gp3
    rep_ar = V(lambda e: e.tensor_copy(g.arep[:], g.stg_ar), [ld_arr], tag="castAr")
    rep_ai = V(lambda e: e.tensor_copy(g.airep[:], g.stg_ai), [ld_air, rep_ar], tag="castAi")
    bias_b = V(lambda e: e.tensor_copy(g.biasb[:], g.pBias[:]), [ld_par])
    rep_d1 = V(lambda e: e.tensor_scalar(g.d1rep[:], g.tD[:, 0:DM], 1.0, None, Alu.add),
               [ld_drep, rep_ai], tag="d1rep")

    kb = g.kb
    m1 = V(lambda e: e.tensor_scalar(kb[0], g.arep[:], g.cth, None, Alu.mult), [rep_ar, ld_cth])
    u1 = V(lambda e: e.scalar_tensor_tensor(kb[1], g.airep[:], g.sth, kb[0], Alu.mult, Alu.add), [rep_ai, ld_sth, m1])
    dr = V(lambda e: e.tensor_scalar(kb[2], kb[1], -1.0, 1.0, Alu.mult, Alu.add), [u1])
    m2 = V(lambda e: e.tensor_scalar(kb[3], g.arep[:], g.sth, None, Alu.mult), [u1])
    di = V(lambda e: e.scalar_tensor_tensor(kb[4], g.airep[:], g.cth, kb[3], Alu.mult, Alu.subtract), [m2])
    v1 = V(lambda e: e.scalar_tensor_tensor(kb[5], g.airep[:], g.sth, kb[0], Alu.mult, Alu.subtract), [di])
    er = V(lambda e: e.tensor_scalar(kb[0], kb[5], 1.0, None, Alu.add), [v1])
    ei = V(lambda e: e.scalar_tensor_tensor(kb[5], g.airep[:], g.cth, kb[3], Alu.mult, Alu.add), [er])
    dsq = V(lambda e: e.tensor_tensor(g.kf1, kb[2], kb[2], Alu.mult), [dr, ei])
    d1a = V(lambda e: e.tensor_tensor(g.kf2, kb[4], kb[4], Alu.mult), [dsq])
    d1b = V(lambda e: e.tensor_tensor(g.kf1, g.kf1, g.kf2, Alu.add), [d1a])
    ln1 = S(lambda e: e.activation(g.sqjunk[0:NF, :], g.kf1, Act.Ln), [d1b], tag="ln1")
    ng1 = V(lambda e: e.tensor_scalar(g.kf2, g.sqjunk[0:NF, :], -1.0, None, Alu.mult), [ln1], tag="ng1")
    ex1 = S(lambda e: e.activation(g.kinv1, g.kf2, Act.Exp), [ng1])
    dsq2 = V(lambda e: e.tensor_tensor(g.kf1, kb[0], kb[0], Alu.mult), [ln1])
    d2a = V(lambda e: e.tensor_tensor(g.kf2, kb[5], kb[5], Alu.mult), [dsq2, ex1])
    d2b = V(lambda e: e.tensor_tensor(g.kf1, g.kf1, g.kf2, Alu.add), [d2a])
    ln2 = S(lambda e: e.activation(g.sqjunk[0:NF, :], g.kf1, Act.Ln), [d2b, ex1], tag="ln2")
    ng2 = V(lambda e: e.tensor_scalar(g.kf2, g.sqjunk[0:NF, :], -1.0, None, Alu.mult), [ln2], tag="ng2")
    ex2 = S(lambda e: e.activation(g.kinv2, g.kf2, Act.Exp), [ng2])
    t_a = V(lambda e: e.tensor_tensor(kb[1], g.grrep, kb[2], Alu.mult), [rep_gr, d2b])
    t_b = V(lambda e: e.tensor_tensor(kb[3], g.girep, kb[4], Alu.mult), [rep_gi, t_a])
    t1r = V(lambda e: e.tensor_tensor(kb[6], kb[1], kb[3], Alu.subtract), [t_b])
    t_c = V(lambda e: e.tensor_tensor(kb[1], g.grrep, kb[4], Alu.mult), [t1r])
    t_d = V(lambda e: e.tensor_tensor(kb[3], g.girep, kb[2], Alu.mult), [t_c])
    t1i = V(lambda e: e.tensor_tensor(kb[7], kb[1], kb[3], Alu.add), [t_d])
    t1rk = V(lambda e: e.tensor_tensor(kb[6], kb[6], g.kinv1, Alu.mult), [t1i, ex1])
    t1ik = V(lambda e: e.tensor_tensor(kb[7], kb[7], g.kinv1, Alu.mult), [t1rk])
    t_e = V(lambda e: e.tensor_tensor(kb[1], g.grrep, kb[0], Alu.mult), [t1ik])
    t_f = V(lambda e: e.tensor_tensor(kb[3], g.girep, kb[5], Alu.mult), [t_e])
    t2r = V(lambda e: e.tensor_tensor(kb[2], kb[1], kb[3], Alu.subtract), [t_f])
    t_g2 = V(lambda e: e.tensor_tensor(kb[1], g.grrep, kb[5], Alu.mult), [t2r])
    t_h = V(lambda e: e.tensor_tensor(kb[3], g.girep, kb[0], Alu.mult), [t_g2])
    t2n = V(lambda e: e.tensor_tensor(kb[4], kb[1], kb[3], Alu.add), [t_h])
    t2rk = V(lambda e: e.tensor_tensor(kb[2], kb[2], g.kinv2, Alu.mult), [t2n, ex2])
    t2nk = V(lambda e: e.tensor_tensor(kb[4], kb[4], g.kinv2, Alu.mult), [t2rk])
    kr_ = V(lambda e: e.tensor_tensor(kb[0], kb[6], kb[2], Alu.add), [t2nk])
    ki_ = V(lambda e: e.tensor_tensor(kb[5], kb[7], kb[4], Alu.subtract), [kr_])
    k_s1a = V(lambda e: e.tensor_scalar(g.ks1[0:NF, :], kb[0], 0.5, None, Alu.mult), [ki_])
    k_s1b = V(lambda e: e.tensor_scalar(g.ks1[NF:128, :], kb[5], 0.5, None, Alu.mult), [k_s1a])
    k_s2a = V(lambda e: e.tensor_scalar(g.ks2[0:NF, :], kb[5], 0.5, None, Alu.mult), [k_s1b])
    k_s2b = V(lambda e: e.tensor_scalar(g.ks2[NF:128, :], kb[0], 0.5, None, Alu.mult), [k_s2a])
    k_done = k_s2b

    castB_halo = S(lambda e: e.activation(g.ubfB[:, 2 * DM:3 * DM], g.u_t[:, DM:2 * DM],
                                          Act.Copy), [ld_uh], tag="castB_halo")
    castB_0 = S(lambda e: e.activation(g.ubfB[:, 0:DM], g.u_t[:, 0:DM], Act.Copy),
                [ld_u[0]], tag="castB_0")

    cW = 1.0 / DM
    T = [SimpleNamespace() for _ in range(NT)]

    def Tat(i):
        return T[i] if 0 <= i < NT else None

    def ga(i, attr):
        t = Tat(i)
        return getattr(t, attr, None) if t is not None else None

    def emit_slot(k):
        a, b, c = k, k - 1, k - 2
        ta, tb, tc = Tat(a), Tat(b), Tat(c)

        if ta is not None and a + 1 < NT:
            ld_u[a + 1] = SD(em_ldu(a + 1),
                             [ga(a - 1, "castB"), ga(a - 1, "tD"),
                              castB_halo if a == 0 else None], tag=f"u{a+1}")

        if ta is not None and a == 0:
            ta.castB = castB_0

        if tb is not None:
            inv = []
            for (cidx, p1t, p2t, pdep) in [(0, g.p1A, g.p2A, tb.pr2A),
                                           (1, g.p1B, g.p2B, tb.pr2B)]:
                col = cidx * STRIDE
                for h in range(2):
                    d0 = [pdep, ld_w1, ld_w2]
                    if cidx == 0 and h == 0:
                        d0.append(ga(b - 1, "p2o"))
                    inv.append(PE(lambda e, col=col, h=h, p1t=p1t: e.matmul(
                        g.psY[col:col + STRIDE, h * 512:(h + 1) * 512], g.w1m,
                        p1t[:, h * 512:(h + 1) * 512], start=True, stop=False,
                        tile_position=(0, col)), d0, tag=f"inv_{b}_{cidx}_{h}a"))
                    inv.append(PE(lambda e, col=col, h=h, p2t=p2t: e.matmul(
                        g.psY[col:col + STRIDE, h * 512:(h + 1) * 512], g.w2m,
                        p2t[:, h * 512:(h + 1) * 512], start=False, stop=True,
                        tile_position=(0, col)), [pdep], tag=f"inv_{b}_{cidx}_{h}b"))
            tb.inv = inv[-1]

        if tb is not None:
            par_ = b % 2
            usl = slice(par_ * DM, (par_ + 1) * DM)
            tD_t = g.tD[:, usl]
            resid_t = g.resid[:, usl]
            scol = slice(par_, par_ + 1)
            rg = [ga(b - 2, "x1b"), ga(b - 2, "p3")]
            tb.p2o = V(lambda e, o=resid_t, t_=tD_t, sc=scol: e.scalar_tensor_tensor(
                o, t_, 0.0, g.psY[:], Alu.add, Alu.add, accum_out=g.rsum[:, sc]),
                [tb.inv, tb.tD] + rg, tag=f"p2_{b}")
            tb.st1 = V(lambda e, sc=scol: e.tensor_scalar(
                g.st_m[:, sc], g.rsum[:, sc], cW, None, Alu.mult),
                [tb.p2o, ga(b - 2, "sq5")], tag=f"st1_{b}")
            tb.rmo = V(lambda e, s_=resid_t, sc=scol: e.tensor_scalar(
                g.rm[:], s_, g.st_m[:, sc], None, Alu.subtract),
                [tb.st1, w_qd[6] if b == 0 else None], tag=f"rm_{b}")
            tb.q1 = V(lambda e: e.tensor_reduce(
                g.bmax[:], g.rm[:].rearrange("p (a b) -> p a b", a=32, b=32),
                AxX, Alu.max, apply_absolute_value=True), [tb.rmo], tag=f"q1_{b}")

        if ta is not None:
            psl = ((a - 1) % 3) * DM
            csl = (a % 3) * DM
            fwdA = []
            for h in range(2):
                d0 = ([ta.castB, ga(a - 1, "evB"), ld_fsalo, ld_fsahi]
                      + ([castB_halo] if a == 0 else [])) if h == 0 else []
                fwdA.append(PE(lambda e, h=h, psl=psl: e.matmul(
                    g.psF[:, h * 512:(h + 1) * 512], g.fsalo,
                    g.ubfB[:, psl + h * 512:psl + (h + 1) * 512],
                    start=True, stop=False), d0, tag=f"fwdA_{a}_{h}a"))
                fwdA.append(PE(lambda e, h=h, csl=csl: e.matmul(
                    g.psF[:, h * 512:(h + 1) * 512], g.fsahi,
                    g.ubfB[:, csl + h * 512:csl + (h + 1) * 512],
                    start=False, stop=True), [], tag=f"fwdA_{a}_{h}b"))
            ta.fwdA = fwdA[-1]

        if tb is not None:
            par_ = b % 2
            resid_t = g.resid[:, par_ * DM:(par_ + 1) * DM]
            scol = slice(par_, par_ + 1)
            stc = lambda cc: g.st_t1[:, par_ * 6 + cc:par_ * 6 + cc + 1]
            st2 = V(lambda e, sc=scol, c0=stc(0): e.tensor_tensor(
                c0, g.st_m[:, sc], g.st_m[:, sc], Alu.mult), [tb.st1])
            tb.p3 = S(lambda e, s_=resid_t, sc=scol: e.activation(
                g.sqjunk[:], s_, Act.Square, accum_out=g.rsq[:, sc]),
                [tb.p2o, w_tw[5] if b == 0 else None], tag=f"p3_{b}")
        if ta is not None:
            ta.evA = S(lambda e: e.activation(g.uevA[:], g.psF[:], Act.Copy),
                       [ta.fwdA, ga(a - 1, "pr2A")], tag=f"evA_{a}")
        if tb is not None:
            par_ = b % 2
            scol = slice(par_, par_ + 1)
            stc = lambda cc: g.st_t1[:, par_ * 6 + cc:par_ * 6 + cc + 1]
            st3 = V(lambda e, sc=scol, c0=stc(0), c1=stc(1): e.scalar_tensor_tensor(
                c1, g.rsq[:, sc], cW, c0, Alu.mult, Alu.subtract),
                [st2, tb.p3], tag=f"st3_{b}")
            tb.st4 = V(lambda e, c1=stc(1): e.tensor_scalar(c1, c1, EPS, None, Alu.add), [st3])
            tb.sq5 = S(lambda e, c1=stc(1), c2=stc(2): e.activation(c2, c1, Act.Sqrt),
                       [tb.st4, tb.p3], tag=f"sqrt_{b}")
        if ta is not None and a + 1 < NT:
            nsl = (a + 1) % 3
            usl = slice(((a + 1) % 2) * DM, ((a + 1) % 2 + 1) * DM)
            T[a + 1].castB = S(lambda e, o=g.ubfB[:, nsl * DM:(nsl + 1) * DM],
                               s_=g.u_t[:, usl]: e.activation(o, s_, Act.Copy),
                               [ld_u[a + 1], ga(a - 1, "fwdA"), ga(a - 2, "fwdB")],
                               tag=f"castB_{a+1}")
        if tc is not None:
            par_ = c % 2
            xd_t = g.xd[:, par_ * DM:(par_ + 1) * DM]
            psT_t = g.psT[:, par_ * DM:(par_ + 1) * DM]
            trs = []
            for cc in range(8):
                d0 = ([tc.q7, ld_id, ga(c - 2, "evT"), w_done if c < 2 else None]
                      if cc == 0 else [tc.q7])
                trs.append(PE(lambda e, cc=cc, s_=xd_t, o=psT_t: e.transpose(
                    o[:, cc * 128:(cc + 1) * 128], s_[:, cc * 128:(cc + 1) * 128],
                    g.identb), d0, tag=f"xtr_{c}_{cc}"))
            tc.trs = trs[-1]

        if tc is not None:
            par_ = c % 2
            xdT_t = g.xdT[:, par_ * DM:(par_ + 1) * DM]
            psT_t = g.psT[:, par_ * DM:(par_ + 1) * DM]
            tc.evT = S(lambda e, o=xdT_t, s_=psT_t: e.activation(o, s_, Act.Copy),
                       [tc.trs, ga(c - 2, "ffn")], tag=f"evT_{c}")

        if tb is not None:
            par_ = b % 2
            rsl = slice(par_ * 32, (par_ + 1) * 32)
            scol = slice(par_, par_ + 1)
            stc = lambda cc: g.st_t1[:, par_ * 6 + cc:par_ * 6 + cc + 1]
            tb.q2b = V(lambda e: e.tensor_scalar(g.bmax[:], g.bmax[:], 7.5e-30, None, Alu.max),
                    [tb.q1], tag=f"cl_{b}")
            q2b = tb.q2b
            q2c = V(lambda e: e.reciprocal(g.binv[:], g.bmax[:]), [q2b])
            tb.q2e = V(lambda e, rs=rsl: e.tensor_scalar(
                g.rr8B[:, rs], g.binv[:], 60.0, None, Alu.mult),
                [q2c, ga(b - 1, "tp")], tag=f"rr8_{b}")

        if ta is not None:
            ta.pr1A = V(lambda e: e.tensor_tensor(g.p1A[:], g.uevA[:], g.ks1[:], Alu.mult),
                        [ta.evA, ga(a - 1, "p2o")], tag=f"p1A_{a}")
            ta.pr2A = V(lambda e: e.tensor_tensor(g.p2A[:], g.uevA[:], g.ks2[:], Alu.mult),
                        [ta.pr1A], tag=f"p2A_{a}")

        if tb is not None:
            par_ = b % 2
            rsl = slice(par_ * 32, (par_ + 1) * 32)
            scol = slice(par_, par_ + 1)
            stc = lambda cc: g.st_t1[:, par_ * 6 + cc:par_ * 6 + cc + 1]
            q2b = tb.q2b
            rc6 = V(lambda e, c2=stc(2), c3=stc(3): e.reciprocal(c3, c2), [tb.sq5])
            nr1 = V(lambda e, c3=stc(3), c4=stc(4): e.tensor_tensor(c4, c3, c3, Alu.mult), [rc6])
            nr2 = V(lambda e, c1=stc(1), c4=stc(4), c5=stc(5): e.tensor_tensor(c5, c1, c4, Alu.mult), [nr1])
            nr3 = V(lambda e, c5=stc(5): e.tensor_scalar(c5, c5, -0.5, 1.5, Alu.mult, Alu.add), [nr2])
            tb.nr4 = V(lambda e, sc=scol, c3=stc(3), c5=stc(5): e.tensor_tensor(
                g.st_r[:, sc], c3, c5, Alu.mult), [nr3], tag=f"r1_{b}")
            tb.q2f = V(lambda e, sc=scol, rs=rsl: e.tensor_scalar(
                g.sbB[:, rs], g.bmax[:], g.st_r[:, sc], 1.0 / 60.0, Alu.mult, Alu.mult),
                [q2b, tb.nr4, ga(b - 2, "q7")], tag=f"sb_{b}")

        if ta is not None:
            par_ = a % 2
            usl = slice(par_ * DM, (par_ + 1) * DM)
            ta.tD = G(lambda e, o=g.tD[:, usl], s_=g.u_t[:, usl]: e.tensor_tensor(
                o, s_, g.d1rep[:], Alu.mult),
                [ld_u[a], rep_d1, ga(a - 2, "p2o")], tag=f"tD_{a}")
        if tb is not None:
            par_ = b % 2
            rsl = slice(par_ * 32, (par_ + 1) * 32)
            rrv = g.rr8B[:, rsl].unsqueeze(2).broadcast_to([128, 32, 32])
            tb.tp = G(lambda e, rrv=rrv: e.tensor_tensor(
                g.tq[:].rearrange("p (a b) -> p a b", a=32, b=32),
                g.rm[:].rearrange("p (a b) -> p a b", a=32, b=32), rrv, Alu.mult),
                [tb.q2e, ga(b - 1, "qd"), ta.pr2A if ta is not None else None,
                 w_qd[7] if b == 0 else None], tag=f"tp_{b}")

        if ta is not None:
            ubfB_t = g.ubfB[:, (a % 3) * DM:(a % 3 + 1) * DM]
            fwdB = []
            for h in range(2):
                d0 = [ta.castB, ta.evA, ld_fs] if h == 0 else []
                fwdB.append(PE(lambda e, h=h, s_=ubfB_t: e.matmul(
                    g.psF[:, h * 512:(h + 1) * 512], g.fstack,
                    s_[:, h * 512:(h + 1) * 512], start=True, stop=True), d0,
                    tag=f"fwdB_{a}_{h}"))
            ta.fwdB = fwdB[-1]

        if ta is not None:
            ta.evB = S(lambda e: e.activation(g.uevB[:], g.psF[:], Act.Copy),
                       [ta.fwdB, ga(a - 1, "pr2B")], tag=f"evB_{a}")

        if tc is not None:
            par_ = c % 2
            usl = slice(par_ * DM, (par_ + 1) * DM)
            xdT_t = g.xdT[:, usl]
            x1b_t = g.x1b[:, usl]
            ffn = []
            for h in range(2):
                d0 = [ones_set, bias_b, ga(c - 1, "zev")] if h == 0 else []
                ffn.append(PE(lambda e, h=h: e.matmul(
                    g.psZ[:, h * 512:(h + 1) * 512], g.ones1[:],
                    g.biasb[:, h * 512:(h + 1) * 512], start=True, stop=False), d0,
                    tag=f"binit_{c}_{h}"))
            for kc in range(8):
                for h in range(2):
                    ffn.append(PE(lambda e, kc=kc, h=h, s_=xdT_t: e.matmul(
                        g.psZ[:, h * 512:(h + 1) * 512], s_[:, kc * 128:(kc + 1) * 128],
                        g.wdT[:, kc * DM + h * 512: kc * DM + (h + 1) * 512],
                        start=False, stop=False), [tc.evT], tag=f"ffn_{c}_{kc}_{h}"))

            for h in range(2):
                ffn.append(PE(lambda e, h=h, s_=x1b_t: e.matmul(
                    g.psZ[:, h * 512:(h + 1) * 512], g.identb,
                    s_[:, h * 512:(h + 1) * 512], start=False, stop=True),
                    [tc.x1b, ld_id], tag=f"zadd_{c}_{h}"))
            tc.ffn = ffn[-1]

        if tb is not None:
            qz = V(lambda e: e._custom_dve(
                QZ_OP, out=g.qi[:], in0=g.tq[:], s0=256.0, s1=0.5, imm2=1024.0),
                [tb.tp, ga(b - 1, "qd")], tag=f"qz_{b}")
            tb.qd = V(lambda e: e._custom_dve(
                QN_OP, out=g.qn[:], in0=g.tq[:], in1=g.qi[:], s0=256.0, s1=2.0, imm2=1024.0),
                [qz, ga(b - 1, "q7")], tag=f"qd_{b}")

        if ta is not None:
            ta.pr1B = V(lambda e: e.tensor_tensor(g.p1B[:], g.uevB[:], g.ks1[:], Alu.mult),
                        [ta.evB], tag=f"p1B_{a}")
            ta.pr2B = V(lambda e: e.tensor_tensor(g.p2B[:], g.uevB[:], g.ks2[:], Alu.mult),
                        [ta.pr1B], tag=f"p2B_{a}")

        if tb is not None:
            par_ = b % 2
            xd_t = g.xd[:, par_ * DM:(par_ + 1) * DM]
            rsl = slice(par_ * 32, (par_ + 1) * 32)
            sbv = g.sbB[:, rsl].unsqueeze(2).broadcast_to([128, 32, 32])
            tb.q7 = G(lambda e, o=xd_t, sbv=sbv: e.tensor_tensor(
                o.rearrange("p (a b) -> p a b", a=32, b=32),
                g.qn[:].rearrange("p (a b) -> p a b", a=32, b=32), sbv, Alu.mult),
                [tb.qd, tb.q2f, ta.pr2B if ta is not None else None,
                 ga(b - 2, "evT")], tag=f"q7_{b}")

        if tb is not None:
            par_ = b % 2
            usl = slice(par_ * DM, (par_ + 1) * DM)
            x1b_t = g.x1b[:, usl]
            resid_t = g.resid[:, usl]
            scol = slice(par_, par_ + 1)
            stc = lambda cc: g.st_t1[:, par_ * 6 + cc:par_ * 6 + cc + 1]
            nmr = V(lambda e, sc=scol, c0=stc(0): e.tensor_scalar(
                c0, g.st_m[:, sc], g.st_r[:, sc], -1.0, Alu.mult, Alu.mult),
                [tb.nr4], tag=f"nmr_{b}")
            tb.x1b = S(lambda e, o=x1b_t, s_=resid_t, sc=scol, c0=stc(0): e.activation(
                o, s_, Act.Identity, bias=c0, scale=g.st_r[:, sc]),
                [nmr, ga(b - 2, "ffn")], tag=f"x1b_{b}")
        if tc is not None:
            par_ = c % 2
            usl = slice(par_ * DM, (par_ + 1) * DM)
            z_t = g.z[:, usl]
            obuf_t = g.obuf[:, usl]
            scol = slice(par_, par_ + 1)
            stc = lambda cc: g.st_t2[:, par_ * 6 + cc:par_ * 6 + cc + 1]
            tc.zev = S(lambda e, o=z_t, sc=scol: e.activation(
                o, g.psZ[:], Act.Copy, accum_out=g.zsum[:, sc]),
                [tc.ffn, ga(c - 2, "oap")], tag=f"zev_{c}")
            z3 = S(lambda e, s_=z_t, sc=scol: e.activation(
                g.sqjunk[:], s_, Act.Square, accum_out=g.zsq[:, sc]), [tc.zev], tag=f"zsq_{c}")
            w1_ = V(lambda e, sc=scol: e.tensor_scalar(
                g.st2_m[:, sc], g.zsum[:, sc], cW, None, Alu.mult),
                [tc.zev, ga(c - 2, "oap"), ga(c - 2, "w5")], tag=f"w1_{c}")
            w2_ = V(lambda e, sc=scol, c0=stc(0): e.tensor_tensor(
                c0, g.st2_m[:, sc], g.st2_m[:, sc], Alu.mult), [w1_])
            w3_ = V(lambda e, sc=scol, c0=stc(0), c1=stc(1): e.scalar_tensor_tensor(
                c1, g.zsq[:, sc], cW, c0, Alu.mult, Alu.subtract), [w2_, z3], tag=f"w3_{c}")
            w4_ = V(lambda e, c1=stc(1): e.tensor_scalar(c1, c1, EPS, None, Alu.add), [w3_])
            tc.w5 = S(lambda e, c1=stc(1), c2=stc(2): e.activation(c2, c1, Act.Sqrt),
                      [w4_, z3], tag=f"sqrt2_{c}")
            w6_ = V(lambda e, c2=stc(2), c3=stc(3): e.reciprocal(c3, c2), [tc.w5])
            x1_ = V(lambda e, c3=stc(3), c4=stc(4): e.tensor_tensor(c4, c3, c3, Alu.mult), [w6_])
            x2_ = V(lambda e, c1=stc(1), c4=stc(4), c5=stc(5): e.tensor_tensor(c5, c1, c4, Alu.mult), [x1_])
            x3_ = V(lambda e, c5=stc(5): e.tensor_scalar(c5, c5, -0.5, 1.5, Alu.mult, Alu.add), [x2_])
            x4_ = V(lambda e, sc=scol, c3=stc(3), c5=stc(5): e.tensor_tensor(
                g.st2_r[:, sc], c3, c5, Alu.mult), [x3_])
            x5_ = V(lambda e, sc=scol, c0=stc(0): e.tensor_scalar(
                c0, g.st2_m[:, sc], g.st2_r[:, sc], -1.0, Alu.mult, Alu.mult),
                [x4_], tag=f"x5_{c}")
            tc.oap = S(lambda e, o=obuf_t, s_=z_t, sc=scol, c0=stc(0): e.activation(
                o, s_, Act.Identity, bias=c0, scale=g.st2_r[:, sc]),
                [x5_, z3, ga(c - 2, "ost")], tag=f"oap_{c}")
            tc.ost = GD(lambda e, cc=c, s_=obuf_t: e.dma_start(
                out=d_out[cc * 128:(cc + 1) * 128, :], in_=s_), [tc.oap], tag=f"ost_{c}")


    for k in range(NT + 2):
        emit_slot(k)

    final_store = T[NT - 1].ost


    watermark = {}

    def emit_stream(eng_name, ops, extra_final=None):
        def body(eng):
            wm = watermark.setdefault(eng_name, {})
            for o in ops:
                needs = {}
                if o.inc == 16:
                    needs[o.engine] = o.val - 16
                for d in o.deps:
                    if d.engine == o.engine and o.engine == "pe":
                        continue
                    needs[d.engine] = max(needs.get(d.engine, 0), d.val)
                for se, val in sorted(needs.items()):
                    if val <= 0:
                        continue
                    if eng_name != "pe" and wm.get(se, -1) >= val:
                        continue
                    eng.wait_ge(sems[se], val)
                    wm[se] = val
                inst = o.fn(eng)
                inst.then_inc(sems[o.engine], o.inc)
            if extra_final is not None:
                eng.wait_ge(sems[extra_final[0]], extra_final[1])
        return body

    gp_ops = [o for o in P.all if o.engine in ("gd", "gc")]
    s_ops = [o for o in P.all if o.engine in ("s", "sd")]

    with nc.Block() as block:
        block.sync(emit_stream("sync", P.streams.get("sync", [])))
        block.vector(emit_stream("v", P.streams.get("v", [])))
        block.scalar(emit_stream("s", s_ops))
        block.tensor(emit_stream("pe", P.streams.get("pe", [])))
        block.gpsimd(emit_stream("gp", gp_ops,
                                 extra_final=("gd", final_store.val)))

    es.close()
    nc.finalize()
    return nc


_NC_CACHE = {}


def kernel(L0_attn_u, L0_attn_A_real, L0_attn_A_imag, L0_attn_B_real,
           L0_attn_B_imag, L0_attn_C_real, L0_attn_C_imag, L0_attn_D,
           L0_ffn_weight, L0_ffn_bias):
    u = np.ascontiguousarray(np.asarray(L0_attn_u, dtype=np.float32))
    par = np.ascontiguousarray(np.stack(
        [np.asarray(x, np.float32).reshape(-1) for x in
         (L0_attn_A_real, L0_attn_A_imag, L0_attn_B_real, L0_attn_B_imag,
          L0_attn_C_real, L0_attn_C_imag, L0_attn_D, L0_ffn_bias)], axis=0))
    w = np.ascontiguousarray(np.asarray(L0_ffn_weight, np.float32))
    consts = _host_consts()

    if "nc" not in _NC_CACHE:
        _NC_CACHE["nc"] = build_nc()
    nc = _NC_CACHE["nc"]

    in_maps = make_in_maps(u, par, w, consts)
    res = run_bass_kernel_spmd(nc, in_maps, core_ids=list(range(NCORES)))
    out = np.empty((B, L, DM), np.float32)
    for c in range(NCORES):
        b, h = c // 2, c % 2
        out[b, h * TOK:(h + 1) * TOK, :] = np.asarray(res.results[c]["out"])
    return out


def make_in_maps(u, par, w, consts):
    in_maps = []
    for c in range(NCORES):
        b, h = c // 2, c % 2
        usl = np.ascontiguousarray(u[b, h * TOK:(h + 1) * TOK, :])
        halo = np.zeros((128, DM), np.float32)
        if h == 1:
            halo = np.concatenate(
                [np.zeros((STRIDE, DM), np.float32), u[b, TOK - STRIDE:TOK, :]], axis=0)
        halo = np.ascontiguousarray(halo)
        m = {"u": usl, "uhalo": halo, "par": par, "w": w}
        rep = lambda row, n: np.ascontiguousarray(np.broadcast_to(row.reshape(1, DM), (n, DM)))
        m["crr"] = rep(par[4], NF); m["cir"] = rep(par[5], NF)
        m["brr"] = rep(par[2], NF); m["bir"] = rep(par[3], NF)
        m["arr"] = rep(par[0], NF); m["air"] = rep(par[1], NF)
        m["drep"] = rep(par[6], 128)
        m.update(consts)
        in_maps.append(m)
    return in_maps


# version 32 (final)
# speedup vs baseline: 1.0550x; 1.0550x over previous
import numpy as np
import ml_dtypes
from types import SimpleNamespace
from contextlib import ExitStack

import concourse.bass as bass
import concourse.mybir as mybir
from concourse.bacc import Bacc
from concourse.bass_utils import run_bass_kernel_spmd
import concourse.dve_ops as dve_ops
from concourse.dve_op import DveOp
from concourse.dve_spec import (Spec, Src0, Src1, C0, C1, C2, One, select, sq,
                                lower)

F32 = mybir.dt.float32
BF16 = mybir.dt.bfloat16
I16 = mybir.dt.int16
Act = mybir.ActivationFunctionType
Alu = mybir.AluOpType
AxX = mybir.AxisListType.X

B, L, DM = 4, 4096, 1024
NCORES = 8
TOK = 2048
NT = TOK // 128
NDFT = 128
NF = NDFT // 2
STRIDE = 64
EPS = 1e-5
BF = ml_dtypes.bfloat16

_q2 = sq(Src0)
_QZ_SPEC = Spec(body=Src0 * select(_q2 < C0, One, select(_q2 < C2, C1, C1 * C1)))
_QN_SPEC = Spec(body=Src1 * select(_q2 < C0, One, select(_q2 < C2, C1, C1 * C1)))


def _qz_ref(in0, in1, s0, s1, imm2):
    t = np.asarray(in0)
    q2 = t * t
    s = np.where(q2 < s0, 1.0, np.where(q2 < imm2, s1, s1 * s1))
    return np.rint(t * s)


def _qn_ref(in0, in1, s0, s1, imm2):
    t = np.asarray(in0)
    q2 = t * t
    iv = np.where(q2 < s0, 1.0, np.where(q2 < imm2, s1, s1 * s1))
    return np.asarray(in1) * iv


def _register_op(name, spec, reference):
    for o in dve_ops.OPS:
        if o.name == name:
            return o
    compiled = lower(spec, ver="v3", perf_en=False, name=name)
    op = DveOp(name, spec, subdim=False, uops_sha={"v3": compiled.sha("v3")},
               reference=reference)
    dve_ops.OPS.append(op)
    row = dve_ops._CUSTOM_DVE_ROW_BASE + len(dve_ops.OPS) - 1
    assert row < 0x20, "custom DVE row budget exceeded"
    dve_ops._SUB_OPCODE_FOR_NAME[name] = row
    dve_ops.CUSTOM_DVE_SPECS[name] = spec
    dve_ops.CUSTOM_DVE_REFS[name] = reference
    return op


QZ_OP = _register_op("MK_QZ", _QZ_SPEC, _qz_ref)
QN_OP = _register_op("MK_QN", _QN_SPEC, _qn_ref)


def _host_consts():
    f = np.arange(NF, dtype=np.float64)
    j = np.arange(NDFT, dtype=np.float64)
    th = 2.0 * np.pi * (f + 0.5) / NDFT
    ang = np.outer(j, th)
    fstack = np.concatenate([np.cos(ang), -np.sin(ang)], axis=1)
    jo = np.arange(STRIDE, NDFT, dtype=np.float64)
    angi = np.outer(th, jo)
    wc = (2.0 / NDFT) * np.cos(angi)
    ws = -(2.0 / NDFT) * np.sin(angi)
    w1 = np.concatenate([wc, -wc], axis=0)
    w2 = np.concatenate([ws, ws], axis=0)
    z64 = np.zeros((64, 128), np.float64)
    fsa_lo = np.concatenate([z64, fstack[0:64]], axis=0)
    fsa_hi = np.concatenate([fstack[64:128], z64], axis=0)
    zpad = np.zeros((64, 640 - 512 - 128), np.float64)
    w12 = np.concatenate([np.concatenate([w1, w2], axis=1)], axis=1)
    pack = np.concatenate([fstack, fsa_lo, fsa_hi, np.eye(128), w12], axis=1)
    ths = np.stack([np.cos(th), np.sin(th)], axis=1)
    return {
        "c_pack": np.ascontiguousarray(pack.astype(BF)),
        "c_ths": np.ascontiguousarray(ths.astype(np.float32)),
    }


class Plan:
    def __init__(self):
        self.streams = {}
        self.vals = {}
        self.all = []

    def add(self, engine, fn, deps=(), inc=1, tag=""):
        val = self.vals.get(engine, 0) + inc
        self.vals[engine] = val
        o = SimpleNamespace(engine=engine, fn=fn,
                            deps=[d for d in deps if d is not None],
                            inc=inc, val=val, tag=tag)
        self.streams.setdefault(engine, []).append(o)
        self.all.append(o)
        return o

    def dma(self, engine, fn, deps=(), tag=""):
        return self.add(engine, fn, deps, inc=16, tag=tag)


def build_nc():
    nc = Bacc("TRN2", target_bir_lowering=False)
    g = SimpleNamespace()

    d_u = nc.dram_tensor("u", [TOK, DM], F32, kind="ExternalInput").ap()
    d_uh = nc.dram_tensor("uhalo", [128, DM], F32, kind="ExternalInput").ap()
    d_par = nc.dram_tensor("par", [8, DM], F32, kind="ExternalInput").ap()
    d_crr = nc.dram_tensor("crr", [NF, DM], F32, kind="ExternalInput").ap()
    d_cir = nc.dram_tensor("cir", [NF, DM], F32, kind="ExternalInput").ap()
    d_brr = nc.dram_tensor("brr", [NF, DM], F32, kind="ExternalInput").ap()
    d_bir = nc.dram_tensor("bir", [NF, DM], F32, kind="ExternalInput").ap()
    d_arr = nc.dram_tensor("arr", [NF, DM], F32, kind="ExternalInput").ap()
    d_air = nc.dram_tensor("air", [NF, DM], F32, kind="ExternalInput").ap()
    d_drep = nc.dram_tensor("drep", [128, DM], F32, kind="ExternalInput").ap()
    d_w = nc.dram_tensor("w", [DM, DM], F32, kind="ExternalInput").ap()
    d_pack = nc.dram_tensor("c_pack", [128, 640], BF16, kind="ExternalInput").ap()
    d_ths = nc.dram_tensor("c_ths", [NF, 2], F32, kind="ExternalInput").ap()
    d_out = nc.dram_tensor("out", [TOK, DM], F32, kind="ExternalOutput").ap()

    es = ExitStack()

    def sb(name, shape, dt):
        t = es.enter_context(nc.sbuf_tensor(name, shape, dt))
        setattr(g, name, t)
        return t

    def psum(name, shape, dt=F32):
        t = es.enter_context(nc.psum_tensor(name, shape, dt))
        setattr(g, name, t)
        return t

    sb("pBias", [1, DM], F32)
    sb("packc", [128, 640], BF16)
    sb("thsc", [NF, 2], F32)
    g.fstack = g.packc[:, 0:128]
    g.fsalo = g.packc[:, 128:256]
    g.fsahi = g.packc[:, 256:384]
    g.identb = g.packc[:, 384:512]
    g.w1m = g.packc[:, 512:576]
    g.w2m = g.packc[:, 576:640]
    g.cth = g.thsc[:, 0:1]
    g.sth = g.thsc[:, 1:2]
    sb("ones1", [1, 128], BF16)
    sb("biasb", [1, DM], BF16)
    sb("d1rep", [128, DM], F32)
    sb("ks1", [128, DM], BF16)
    sb("ks2", [128, DM], BF16)
    sb("w_bmax", [128, 128], F32)
    sb("w_sb", [128, 64], BF16)
    sb("wdT", [128, 8 * DM], BF16)
    sb("u_t", [128, 2 * DM], F32)
    sb("ubfB", [128, 3 * DM], BF16)
    sb("uevA", [128, DM], BF16)
    sb("uevB", [128, DM], BF16)
    sb("p1A", [128, DM], BF16)
    sb("p2A", [128, DM], BF16)
    sb("p1B", [128, DM], BF16)
    sb("p2B", [128, DM], BF16)
    sb("tD", [128, 2 * DM], F32)
    sb("resid", [128, 2 * DM], F32)
    sb("rsum", [128, 2], F32)
    sb("rsq", [128, 2], F32)
    sb("sqjunk", [128, DM], F32)
    sb("st_m", [128, 2], F32)
    sb("st_r", [128, 2], F32)
    sb("st_t1", [128, 12], F32)
    sb("st_t2", [128, 12], F32)
    sb("st2_m", [128, 2], F32)
    sb("st2_r", [128, 2], F32)
    sb("rm", [128, DM], F32)
    sb("bmax", [128, 32], F32)
    sb("binv", [128, 32], F32)
    sb("r8B", [128, 32], F32)
    sb("rr8B", [128, 64], F32)
    sb("sbB", [128, 64], BF16)
    sb("tq", [128, DM], F32)
    sb("qi", [128, DM], I16)
    sb("iv", [128, DM], BF16)
    sb("qn", [128, DM], BF16)
    sb("xd", [128, 2 * DM], BF16)
    sb("x1b", [128, 2 * DM], BF16)
    sb("xdT", [128, 2 * DM], BF16)
    sb("z", [128, 2 * DM], F32)
    sb("zsum", [128, 2], F32)
    sb("zsq", [128, 2], F32)
    sb("obuf", [128, 2 * DM], F32)

    sb("kbuf", [NF, 8 * DM], BF16)
    sb("arep", [NF, DM], BF16)
    sb("airep", [NF, DM], BF16)
    g.grrep = g.x1b[0:NF, 0:DM]
    g.girep = g.x1b[0:NF, DM:2 * DM]
    g.kf1 = g.z[0:NF, 0:DM]
    g.kf2 = g.z[0:NF, DM:2 * DM]
    g.kinv1 = g.resid[0:NF, 0:DM]
    g.kinv2 = g.resid[0:NF, DM:2 * DM]
    g.wstage = g.obuf
    g.w_qi = g.qi[:]
    g.w_qn = g.qn[:]
    g.w_d = g.xd[:, DM:2 * DM]
    g.kb = [g.kbuf[:, i * DM:(i + 1) * DM] for i in range(8)]
    g.stg_cr = g.z[0:NF, 0:DM]
    g.stg_br = g.z[0:NF, DM:2 * DM]
    g.stg_ci = g.resid[0:NF, 0:DM]
    g.stg_bi = g.resid[0:NF, DM:2 * DM]
    g.stg_ar = g.tD[0:NF, DM:2 * DM]
    g.stg_ai = g.d1rep[0:NF, 0:DM]

    psum("psF", [128, DM])
    psum("psY", [128, DM])
    psum("psT", [128, 2 * DM], BF16)
    psum("psZ", [128, DM])

    sems = {nm: es.enter_context(nc.semaphore("sem_" + nm))
            for nm in ["sync", "gd", "pe", "v", "s", "gc", "sd"]}

    P = Plan()
    V = lambda fn, deps=(), tag="": P.add("v", fn, deps, tag=tag)
    S = lambda fn, deps=(), tag="": P.add("s", fn, deps, tag=tag)
    G = lambda fn, deps=(), tag="": P.add("gc", fn, deps, tag=tag)
    PE = lambda fn, deps=(), tag="": P.add("pe", fn, deps, tag=tag)
    GD = lambda fn, deps=(), tag="": P.dma("gd", fn, deps, tag=tag)
    SY = lambda fn, deps=(), tag="": P.dma("sync", fn, deps, tag=tag)
    SD = lambda fn, deps=(), tag="": P.dma("sd", fn, deps, tag=tag)

    ld_pack = SY(lambda e: e.dma_start(out=g.packc[:], in_=d_pack))
    ld_ths = SY(lambda e: e.dma_start(out=g.thsc[:], in_=d_ths))
    ld_par = SY(lambda e: e.dma_start(out=g.pBias[:], in_=d_par[7:8, :]))
    ld_id = ld_fs = ld_fsalo = ld_fsahi = ld_w1 = ld_w2 = ld_pack
    ld_cth = ld_sth = ld_ths

    ld_u = [None] * NT

    def em_ldu(i):
        return lambda e: e.dma_start(out=g.u_t[:, (i % 2) * DM:(i % 2 + 1) * DM],
                                     in_=d_u[i * 128:(i + 1) * 128, :])

    ld_u[0] = SD(em_ldu(0), tag="u0")
    ld_uh = SD(lambda e: e.dma_start(out=g.u_t[:, DM:2 * DM], in_=d_uh))
    ld_crr = SD(lambda e: e.dma_start(out=g.stg_cr, in_=d_crr))
    ld_brr = SD(lambda e: e.dma_start(out=g.stg_br, in_=d_brr))
    ld_cir = SD(lambda e: e.dma_start(out=g.stg_ci, in_=d_cir))
    ld_bir = SD(lambda e: e.dma_start(out=g.stg_bi, in_=d_bir))
    ld_arr = SD(lambda e: e.dma_start(out=g.stg_ar, in_=d_arr))
    ld_air = SD(lambda e: e.dma_start(out=g.stg_ai, in_=d_air))

    wslot = [g.wstage[:, 0:DM], g.wstage[:, DM:2 * DM], g.sqjunk[:, 0:DM]]

    def em_ldw_lo(r):
        return lambda e: e.dma_start(out=wslot[r % 3][0:64, :],
                                     in_=d_w[r * 128:r * 128 + 64, :])

    def em_ldw_hi(r):
        return lambda e: e.dma_start(out=wslot[r % 3][64:128, :],
                                     in_=d_w[r * 128 + 64:(r + 1) * 128, :])

    w_lo = [None] * 8
    w_hi = [None] * 8
    for _r in range(3):
        w_lo[_r] = GD(em_ldw_lo(_r), tag=f"wlo{_r}")
        w_hi[_r] = SY(em_ldw_hi(_r), tag=f"whi{_r}")
    ld_drep = GD(lambda e: e.dma_start(out=g.tD[:, 0:DM], in_=d_drep))

    ones_set = V(lambda e: e.memset(g.ones1[:], 1.0))

    w_q2c = [None] * 8
    w_q2sb = [None] * 8
    w_tw = [None] * 8
    w_qz = [None] * 8
    w_qd = [None] * 8
    w_q7 = [None] * 8
    w_trs = [None] * 8
    w_ev = [None] * 8

    def w_head(r):
        wst = wslot[r % 3]
        bsl = slice((r % 2) * 32, (r % 2 + 1) * 32)
        rsl = slice(64 + (r % 2) * 32, 64 + (r % 2 + 1) * 32)
        q1 = V(lambda e, wst=wst, bsl=bsl: e.tensor_reduce(
            g.w_bmax[:, bsl], wst.rearrange("p (a b) -> p a b", a=32, b=32),
            AxX, Alu.max, apply_absolute_value=True),
            [w_lo[r], w_hi[r], w_tw[r - 3] if r >= 3 else None], tag=f"wq1_{r}")
        q2a = V(lambda e, bsl=bsl: e.tensor_scalar(g.w_bmax[:, bsl], g.w_bmax[:, bsl], 7.5e-30, None, Alu.max), [q1])
        q2b = V(lambda e, bsl=bsl, rsl=rsl: e.reciprocal(g.w_bmax[:, rsl], g.w_bmax[:, bsl]),
                [q2a, w_tw[r - 2] if r >= 2 else None])
        w_q2c[r] = V(lambda e, rsl=rsl: e.tensor_scalar(g.w_bmax[:, rsl], g.w_bmax[:, rsl], 60.0, None, Alu.mult), [q2b])
        w_q2sb[r] = V(lambda e, bsl=bsl, ssl=slice((r % 2) * 32, (r % 2 + 1) * 32): e.tensor_scalar(
            g.w_sb[:, ssl], g.w_bmax[:, bsl], 1.0 / 60.0, None, Alu.mult),
            [q2a, w_q7[r - 2] if r >= 2 else None], tag=f"wsb_{r}")
        tbuf = g.rm[:] if r % 2 == 0 else g.tq[:]
        r8v = g.w_bmax[:, rsl].unsqueeze(2).broadcast_to([128, 32, 32])
        w_tw[r] = G(lambda e, wst=wst, r8v=r8v, tbuf=tbuf: e.tensor_tensor(
            tbuf.rearrange("p (a b) -> p a b", a=32, b=32),
            wst.rearrange("p (a b) -> p a b", a=32, b=32), r8v, Alu.mult),
            [w_q2c[r], w_qd[r - 2] if r >= 2 else None], tag=f"wtw_{r}")
        if r + 3 < 8:
            w_lo[r + 3] = GD(em_ldw_lo(r + 3), [w_tw[r]], tag=f"wlo{r+3}")
            w_hi[r + 3] = SY(em_ldw_hi(r + 3), [w_tw[r]], tag=f"whi{r+3}")

    def w_tail(r):
        tbuf = g.rm[:] if r % 2 == 0 else g.tq[:]
        ssl = slice((r % 2) * 32, (r % 2 + 1) * 32)
        w_qz[r] = V(lambda e, tbuf=tbuf: e._custom_dve(
            QZ_OP, out=g.w_qi, in0=tbuf, s0=256.0, s1=0.5, imm2=1024.0),
            [w_tw[r], w_qd[r - 1] if r >= 1 else None], tag=f"wqz_{r}")
        w_qd[r] = V(lambda e, tbuf=tbuf: e._custom_dve(
            QN_OP, out=g.w_qn, in0=tbuf, in1=g.w_qi, s0=256.0, s1=2.0, imm2=1024.0),
            [w_qz[r], w_q7[r - 1] if r >= 1 else None], tag=f"wqd_{r}")
        sbv = g.w_sb[:, ssl].unsqueeze(2).broadcast_to([128, 32, 32])
        w_q7[r] = G(lambda e, sbv=sbv: e.tensor_tensor(
            g.w_d[:, 0:DM].rearrange("p (a b) -> p a b", a=32, b=32),
            g.w_qn[:].rearrange("p (a b) -> p a b", a=32, b=32), sbv, Alu.mult),
            [w_qd[r], w_q2sb[r], w_trs[r - 1] if r >= 1 else None], tag=f"wq7_{r}")
        trs = []
        psl = (r % 2) * DM
        for c in range(8):
            d0 = [w_q7[r], ld_id, w_ev[r - 2] if r >= 2 else None] if c == 0 else [w_q7[r]]
            trs.append(PE(lambda e, c=c, psl=psl: e.transpose(
                g.psT[:, psl + c * 128:psl + (c + 1) * 128],
                g.w_d[:, c * 128:(c + 1) * 128],
                g.identb), d0, tag=f"wtr_{r}_{c}"))
        w_trs[r] = trs[-1]
        wdT_dst = g.wdT[:].rearrange("p (c n) -> p c n", c=8, n=DM)[:, :, r * 128:(r + 1) * 128]
        w_ev[r] = S(lambda e, wdT_dst=wdT_dst, psl=psl: e.activation(
            wdT_dst, g.psT[:, psl:psl + DM].rearrange("p (a b) -> p a b", a=8, b=128),
            Act.Copy),
            [w_trs[r], w_ev[r - 1] if r >= 1 else None], tag=f"wev_{r}")

    for r in range(9):
        if r < 8:
            w_head(r)
        if r >= 1:
            w_tail(r - 1)
    w_done = w_ev[7]

    gp1 = V(lambda e: e.tensor_tensor(g.kb[0], g.stg_cr, g.stg_br, Alu.mult), [ld_crr, ld_brr], tag="gp1")
    gp2 = V(lambda e: e.tensor_tensor(g.kb[1], g.stg_ci, g.stg_bi, Alu.mult), [ld_cir, ld_bir], tag="gp2")
    gp3 = V(lambda e: e.tensor_tensor(g.grrep, g.kb[0], g.kb[1], Alu.subtract), [gp1, gp2], tag="gp3")
    gp4 = V(lambda e: e.tensor_tensor(g.kb[0], g.stg_cr, g.stg_bi, Alu.mult), [gp3], tag="gp4")
    gp5 = V(lambda e: e.tensor_tensor(g.kb[1], g.stg_ci, g.stg_br, Alu.mult), [gp4], tag="gp5")
    rep_gi = V(lambda e: e.tensor_tensor(g.girep, g.kb[0], g.kb[1], Alu.add), [gp5], tag="gp6")
    rep_gr = gp3
    rep_ar = V(lambda e: e.tensor_copy(g.arep[:], g.stg_ar), [ld_arr], tag="castAr")
    rep_ai = V(lambda e: e.tensor_copy(g.airep[:], g.stg_ai), [ld_air, rep_ar], tag="castAi")
    bias_b = V(lambda e: e.tensor_copy(g.biasb[:], g.pBias[:]), [ld_par])
    rep_d1 = V(lambda e: e.tensor_scalar(g.d1rep[:], g.tD[:, 0:DM], 1.0, None, Alu.add),
               [ld_drep, rep_ai], tag="d1rep")

    kb = g.kb
    m1 = V(lambda e: e.tensor_scalar(kb[0], g.arep[:], g.cth, None, Alu.mult), [rep_ar, ld_cth])
    u1 = V(lambda e: e.scalar_tensor_tensor(kb[1], g.airep[:], g.sth, kb[0], Alu.mult, Alu.add), [rep_ai, ld_sth, m1])
    dr = V(lambda e: e.tensor_scalar(kb[2], kb[1], -1.0, 1.0, Alu.mult, Alu.add), [u1])
    m2 = V(lambda e: e.tensor_scalar(kb[3], g.arep[:], g.sth, None, Alu.mult), [u1])
    di = V(lambda e: e.scalar_tensor_tensor(kb[4], g.airep[:], g.cth, kb[3], Alu.mult, Alu.subtract), [m2])
    v1 = V(lambda e: e.scalar_tensor_tensor(kb[5], g.airep[:], g.sth, kb[0], Alu.mult, Alu.subtract), [di])
    er = V(lambda e: e.tensor_scalar(kb[0], kb[5], 1.0, None, Alu.add), [v1])
    ei = V(lambda e: e.scalar_tensor_tensor(kb[5], g.airep[:], g.cth, kb[3], Alu.mult, Alu.add), [er])
    dsq = V(lambda e: e.tensor_tensor(g.kf1, kb[2], kb[2], Alu.mult), [dr, ei])
    d1a = V(lambda e: e.tensor_tensor(g.kf2, kb[4], kb[4], Alu.mult), [dsq])
    d1b = V(lambda e: e.tensor_tensor(g.kf1, g.kf1, g.kf2, Alu.add), [d1a])
    ln1 = S(lambda e: e.activation(g.sqjunk[0:NF, :], g.kf1, Act.Ln), [d1b], tag="ln1")
    ng1 = V(lambda e: e.tensor_scalar(g.kf2, g.sqjunk[0:NF, :], -1.0, None, Alu.mult), [ln1], tag="ng1")
    ex1 = S(lambda e: e.activation(g.kinv1, g.kf2, Act.Exp), [ng1])
    dsq2 = V(lambda e: e.tensor_tensor(g.kf1, kb[0], kb[0], Alu.mult), [ln1])
    d2a = V(lambda e: e.tensor_tensor(g.kf2, kb[5], kb[5], Alu.mult), [dsq2, ex1])
    d2b = V(lambda e: e.tensor_tensor(g.kf1, g.kf1, g.kf2, Alu.add), [d2a])
    ln2 = S(lambda e: e.activation(g.sqjunk[0:NF, :], g.kf1, Act.Ln), [d2b, ex1], tag="ln2")
    ng2 = V(lambda e: e.tensor_scalar(g.kf2, g.sqjunk[0:NF, :], -1.0, None, Alu.mult), [ln2], tag="ng2")
    ex2 = S(lambda e: e.activation(g.kinv2, g.kf2, Act.Exp), [ng2])
    t_a = V(lambda e: e.tensor_tensor(kb[1], g.grrep, kb[2], Alu.mult), [rep_gr, d2b])
    t_b = V(lambda e: e.tensor_tensor(kb[3], g.girep, kb[4], Alu.mult), [rep_gi, t_a])
    t1r = V(lambda e: e.tensor_tensor(kb[6], kb[1], kb[3], Alu.subtract), [t_b])
    t_c = V(lambda e: e.tensor_tensor(kb[1], g.grrep, kb[4], Alu.mult), [t1r])
    t_d = V(lambda e: e.tensor_tensor(kb[3], g.girep, kb[2], Alu.mult), [t_c])
    t1i = V(lambda e: e.tensor_tensor(kb[7], kb[1], kb[3], Alu.add), [t_d])
    t1rk = V(lambda e: e.tensor_tensor(kb[6], kb[6], g.kinv1, Alu.mult), [t1i, ex1])
    t1ik = V(lambda e: e.tensor_tensor(kb[7], kb[7], g.kinv1, Alu.mult), [t1rk])
    t_e = V(lambda e: e.tensor_tensor(kb[1], g.grrep, kb[0], Alu.mult), [t1ik])
    t_f = V(lambda e: e.tensor_tensor(kb[3], g.girep, kb[5], Alu.mult), [t_e])
    t2r = V(lambda e: e.tensor_tensor(kb[2], kb[1], kb[3], Alu.subtract), [t_f])
    t_g2 = V(lambda e: e.tensor_tensor(kb[1], g.grrep, kb[5], Alu.mult), [t2r])
    t_h = V(lambda e: e.tensor_tensor(kb[3], g.girep, kb[0], Alu.mult), [t_g2])
    t2n = V(lambda e: e.tensor_tensor(kb[4], kb[1], kb[3], Alu.add), [t_h])
    t2rk = V(lambda e: e.tensor_tensor(kb[2], kb[2], g.kinv2, Alu.mult), [t2n, ex2])
    t2nk = V(lambda e: e.tensor_tensor(kb[4], kb[4], g.kinv2, Alu.mult), [t2rk])
    kr_ = V(lambda e: e.tensor_tensor(kb[0], kb[6], kb[2], Alu.add), [t2nk])
    ki_ = V(lambda e: e.tensor_tensor(kb[5], kb[7], kb[4], Alu.subtract), [kr_])
    k_s1a = V(lambda e: e.tensor_scalar(g.ks1[0:NF, :], kb[0], 0.5, None, Alu.mult), [ki_])
    k_s1b = V(lambda e: e.tensor_scalar(g.ks1[NF:128, :], kb[5], 0.5, None, Alu.mult), [k_s1a])
    k_s2a = V(lambda e: e.tensor_scalar(g.ks2[0:NF, :], kb[5], 0.5, None, Alu.mult), [k_s1b])
    k_s2b = V(lambda e: e.tensor_scalar(g.ks2[NF:128, :], kb[0], 0.5, None, Alu.mult), [k_s2a])
    k_done = k_s2b

    castB_halo = S(lambda e: e.activation(g.ubfB[:, 2 * DM:3 * DM], g.u_t[:, DM:2 * DM],
                                          Act.Copy), [ld_uh], tag="castB_halo")
    castB_0 = S(lambda e: e.activation(g.ubfB[:, 0:DM], g.u_t[:, 0:DM], Act.Copy),
                [ld_u[0]], tag="castB_0")

    cW = 1.0 / DM
    T = [SimpleNamespace() for _ in range(NT)]

    def Tat(i):
        return T[i] if 0 <= i < NT else None

    def ga(i, attr):
        t = Tat(i)
        return getattr(t, attr, None) if t is not None else None

    def emit_slot(k):
        a, b, c = k, k - 1, k - 2
        ta, tb, tc = Tat(a), Tat(b), Tat(c)

        if ta is not None and a + 1 < NT:
            ld_u[a + 1] = SD(em_ldu(a + 1),
                             [ga(a - 1, "castB"), ga(a - 1, "tD"),
                              castB_halo if a == 0 else None], tag=f"u{a+1}")

        if ta is not None and a == 0:
            ta.castB = castB_0

        if tb is not None:
            inv = []
            for (cidx, p1t, p2t, pdep) in [(0, g.p1A, g.p2A, tb.pr2A),
                                           (1, g.p1B, g.p2B, tb.pr2B)]:
                col = cidx * STRIDE
                for h in range(2):
                    d0 = [pdep, ld_w1, ld_w2]
                    if cidx == 0 and h == 0:
                        d0.append(ga(b - 1, "p2o"))
                    inv.append(PE(lambda e, col=col, h=h, p1t=p1t: e.matmul(
                        g.psY[col:col + STRIDE, h * 512:(h + 1) * 512], g.w1m,
                        p1t[:, h * 512:(h + 1) * 512], start=True, stop=False,
                        tile_position=(0, col)), d0, tag=f"inv_{b}_{cidx}_{h}a"))
                    inv.append(PE(lambda e, col=col, h=h, p2t=p2t: e.matmul(
                        g.psY[col:col + STRIDE, h * 512:(h + 1) * 512], g.w2m,
                        p2t[:, h * 512:(h + 1) * 512], start=False, stop=True,
                        tile_position=(0, col)), [pdep], tag=f"inv_{b}_{cidx}_{h}b"))
            tb.inv = inv[-1]

        if tb is not None:
            par_ = b % 2
            usl = slice(par_ * DM, (par_ + 1) * DM)
            tD_t = g.tD[:, usl]
            resid_t = g.resid[:, usl]
            scol = slice(par_, par_ + 1)
            rg = [ga(b - 2, "x1b"), ga(b - 2, "p3")]
            tb.p2o = V(lambda e, o=resid_t, t_=tD_t, sc=scol: e.scalar_tensor_tensor(
                o, t_, 0.0, g.psY[:], Alu.add, Alu.add, accum_out=g.rsum[:, sc]),
                [tb.inv, tb.tD] + rg, tag=f"p2_{b}")
            tb.st1 = V(lambda e, sc=scol: e.tensor_scalar(
                g.st_m[:, sc], g.rsum[:, sc], cW, None, Alu.mult),
                [tb.p2o, ga(b - 2, "sq5")], tag=f"st1_{b}")
            tb.rmo = V(lambda e, s_=resid_t, sc=scol: e.tensor_scalar(
                g.rm[:], s_, g.st_m[:, sc], None, Alu.subtract),
                [tb.st1, w_qd[6] if b == 0 else None], tag=f"rm_{b}")
            tb.q1 = V(lambda e: e.tensor_reduce(
                g.bmax[:], g.rm[:].rearrange("p (a b) -> p a b", a=32, b=32),
                AxX, Alu.max, apply_absolute_value=True), [tb.rmo], tag=f"q1_{b}")

        if ta is not None:
            psl = ((a - 1) % 3) * DM
            csl = (a % 3) * DM
            fwdA = []
            for h in range(2):
                d0 = ([ta.castB, ga(a - 1, "evB"), ld_fsalo, ld_fsahi]
                      + ([castB_halo] if a == 0 else [])) if h == 0 else []
                fwdA.append(PE(lambda e, h=h, psl=psl: e.matmul(
                    g.psF[:, h * 512:(h + 1) * 512], g.fsalo,
                    g.ubfB[:, psl + h * 512:psl + (h + 1) * 512],
                    start=True, stop=False), d0, tag=f"fwdA_{a}_{h}a"))
                fwdA.append(PE(lambda e, h=h, csl=csl: e.matmul(
                    g.psF[:, h * 512:(h + 1) * 512], g.fsahi,
                    g.ubfB[:, csl + h * 512:csl + (h + 1) * 512],
                    start=False, stop=True), [], tag=f"fwdA_{a}_{h}b"))
            ta.fwdA = fwdA[-1]

        if tb is not None:
            par_ = b % 2
            resid_t = g.resid[:, par_ * DM:(par_ + 1) * DM]
            scol = slice(par_, par_ + 1)
            stc = lambda cc: g.st_t1[:, par_ * 6 + cc:par_ * 6 + cc + 1]
            st2 = V(lambda e, sc=scol, c0=stc(0): e.tensor_tensor(
                c0, g.st_m[:, sc], g.st_m[:, sc], Alu.mult), [tb.st1])
            tb.p3 = S(lambda e, s_=resid_t, sc=scol: e.activation(
                g.sqjunk[:], s_, Act.Square, accum_out=g.rsq[:, sc]),
                [tb.p2o, w_tw[5] if b == 0 else None], tag=f"p3_{b}")
        if ta is not None:
            ta.evA = S(lambda e: e.activation(g.uevA[:], g.psF[:], Act.Copy),
                       [ta.fwdA, ga(a - 1, "pr2A")], tag=f"evA_{a}")
        if tb is not None:
            par_ = b % 2
            scol = slice(par_, par_ + 1)
            stc = lambda cc: g.st_t1[:, par_ * 6 + cc:par_ * 6 + cc + 1]
            st3 = V(lambda e, sc=scol, c0=stc(0), c1=stc(1): e.scalar_tensor_tensor(
                c1, g.rsq[:, sc], cW, c0, Alu.mult, Alu.subtract),
                [st2, tb.p3], tag=f"st3_{b}")
            tb.st4 = V(lambda e, c1=stc(1): e.tensor_scalar(c1, c1, EPS, None, Alu.add), [st3])
            tb.sq5 = S(lambda e, c1=stc(1), c2=stc(2): e.activation(c2, c1, Act.Sqrt),
                       [tb.st4, tb.p3], tag=f"sqrt_{b}")
        if ta is not None and a + 1 < NT:
            nsl = (a + 1) % 3
            usl = slice(((a + 1) % 2) * DM, ((a + 1) % 2 + 1) * DM)
            T[a + 1].castB = S(lambda e, o=g.ubfB[:, nsl * DM:(nsl + 1) * DM],
                               s_=g.u_t[:, usl]: e.activation(o, s_, Act.Copy),
                               [ld_u[a + 1], ga(a - 1, "fwdA"), ga(a - 2, "fwdB")],
                               tag=f"castB_{a+1}")
        if tc is not None:
            par_ = c % 2
            xd_t = g.xd[:, par_ * DM:(par_ + 1) * DM]
            psT_t = g.psT[:, par_ * DM:(par_ + 1) * DM]
            trs = []
            for cc in range(8):
                d0 = ([tc.q7, ld_id, ga(c - 2, "evT"), w_done if c < 2 else None]
                      if cc == 0 else [tc.q7])
                trs.append(PE(lambda e, cc=cc, s_=xd_t, o=psT_t: e.transpose(
                    o[:, cc * 128:(cc + 1) * 128], s_[:, cc * 128:(cc + 1) * 128],
                    g.identb), d0, tag=f"xtr_{c}_{cc}"))
            tc.trs4 = trs[3]
            tc.trs = trs[-1]

        if tc is not None:
            par_ = c % 2
            xdT_t = g.xdT[:, par_ * DM:(par_ + 1) * DM]
            psT_t = g.psT[:, par_ * DM:(par_ + 1) * DM]
            tc.evT0 = S(lambda e, o=xdT_t, s_=psT_t: e.activation(
                o[:, 0:512], s_[:, 0:512], Act.Copy),
                [tc.trs4, ga(c - 2, "ffn")], tag=f"evTa_{c}")
            tc.evT = S(lambda e, o=xdT_t, s_=psT_t: e.activation(
                o[:, 512:1024], s_[:, 512:1024], Act.Copy),
                [tc.trs], tag=f"evT_{c}")

        if tb is not None:
            par_ = b % 2
            rsl = slice(par_ * 32, (par_ + 1) * 32)
            scol = slice(par_, par_ + 1)
            stc = lambda cc: g.st_t1[:, par_ * 6 + cc:par_ * 6 + cc + 1]
            tb.q2b = V(lambda e: e.tensor_scalar(g.bmax[:], g.bmax[:], 7.5e-30, None, Alu.max),
                    [tb.q1], tag=f"cl_{b}")
            q2b = tb.q2b
            q2c = V(lambda e: e.reciprocal(g.binv[:], g.bmax[:]), [q2b])
            tb.q2e = V(lambda e, rs=rsl: e.tensor_scalar(
                g.rr8B[:, rs], g.binv[:], 60.0, None, Alu.mult),
                [q2c, ga(b - 1, "tp")], tag=f"rr8_{b}")

        if ta is not None:
            ta.pr1A = V(lambda e: e.tensor_tensor(g.p1A[:], g.uevA[:], g.ks1[:], Alu.mult),
                        [ta.evA, ga(a - 1, "p2o")], tag=f"p1A_{a}")
            ta.pr2A = V(lambda e: e.tensor_tensor(g.p2A[:], g.uevA[:], g.ks2[:], Alu.mult),
                        [ta.pr1A], tag=f"p2A_{a}")

        if tb is not None:
            par_ = b % 2
            rsl = slice(par_ * 32, (par_ + 1) * 32)
            scol = slice(par_, par_ + 1)
            stc = lambda cc: g.st_t1[:, par_ * 6 + cc:par_ * 6 + cc + 1]
            q2b = tb.q2b
            rc6 = V(lambda e, c2=stc(2), c3=stc(3): e.reciprocal(c3, c2), [tb.sq5])
            nr1 = V(lambda e, c3=stc(3), c4=stc(4): e.tensor_tensor(c4, c3, c3, Alu.mult), [rc6])
            nr2 = V(lambda e, c1=stc(1), c4=stc(4), c5=stc(5): e.tensor_tensor(c5, c1, c4, Alu.mult), [nr1])
            nr3 = V(lambda e, c5=stc(5): e.tensor_scalar(c5, c5, -0.5, 1.5, Alu.mult, Alu.add), [nr2])
            tb.nr4 = V(lambda e, sc=scol, c3=stc(3), c5=stc(5): e.tensor_tensor(
                g.st_r[:, sc], c3, c5, Alu.mult), [nr3], tag=f"r1_{b}")
            tb.q2f = V(lambda e, sc=scol, rs=rsl: e.tensor_scalar(
                g.sbB[:, rs], g.bmax[:], g.st_r[:, sc], 1.0 / 60.0, Alu.mult, Alu.mult),
                [q2b, tb.nr4, ga(b - 2, "q7")], tag=f"sb_{b}")

        if ta is not None:
            par_ = a % 2
            usl = slice(par_ * DM, (par_ + 1) * DM)
            ta.tD = G(lambda e, o=g.tD[:, usl], s_=g.u_t[:, usl]: e.tensor_tensor(
                o, s_, g.d1rep[:], Alu.mult),
                [ld_u[a], rep_d1, ga(a - 2, "p2o")], tag=f"tD_{a}")
        if tb is not None:
            par_ = b % 2
            rsl = slice(par_ * 32, (par_ + 1) * 32)
            rrv = g.rr8B[:, rsl].unsqueeze(2).broadcast_to([128, 32, 32])
            tb.tp = G(lambda e, rrv=rrv: e.tensor_tensor(
                g.tq[:].rearrange("p (a b) -> p a b", a=32, b=32),
                g.rm[:].rearrange("p (a b) -> p a b", a=32, b=32), rrv, Alu.mult),
                [tb.q2e, ga(b - 1, "qd"), w_qd[7] if b == 0 else None], tag=f"tp_{b}")

        if ta is not None:
            ubfB_t = g.ubfB[:, (a % 3) * DM:(a % 3 + 1) * DM]
            fwdB = []
            for h in range(2):
                d0 = [ta.castB, ta.evA, ld_fs] if h == 0 else []
                fwdB.append(PE(lambda e, h=h, s_=ubfB_t: e.matmul(
                    g.psF[:, h * 512:(h + 1) * 512], g.fstack,
                    s_[:, h * 512:(h + 1) * 512], start=True, stop=True), d0,
                    tag=f"fwdB_{a}_{h}"))
            ta.fwdB = fwdB[-1]

        if ta is not None:
            ta.evB = S(lambda e: e.activation(g.uevB[:], g.psF[:], Act.Copy),
                       [ta.fwdB, ga(a - 1, "pr2B")], tag=f"evB_{a}")

        if tc is not None:
            par_ = c % 2
            usl = slice(par_ * DM, (par_ + 1) * DM)
            xdT_t = g.xdT[:, usl]
            x1b_t = g.x1b[:, usl]
            ffn = []
            for h in range(2):
                d0 = [ones_set, bias_b, ga(c - 1, "zev")] if h == 0 else []
                ffn.append(PE(lambda e, h=h: e.matmul(
                    g.psZ[:, h * 512:(h + 1) * 512], g.ones1[:],
                    g.biasb[:, h * 512:(h + 1) * 512], start=True, stop=False), d0,
                    tag=f"binit_{c}_{h}"))
            for kc in range(8):
                for h in range(2):
                    ffn.append(PE(lambda e, kc=kc, h=h, s_=xdT_t: e.matmul(
                        g.psZ[:, h * 512:(h + 1) * 512], s_[:, kc * 128:(kc + 1) * 128],
                        g.wdT[:, kc * DM + h * 512: kc * DM + (h + 1) * 512],
                        start=False, stop=False),
                        [tc.evT0 if kc < 4 else tc.evT], tag=f"ffn_{c}_{kc}_{h}"))

            for h in range(2):
                ffn.append(PE(lambda e, h=h, s_=x1b_t: e.matmul(
                    g.psZ[:, h * 512:(h + 1) * 512], g.identb,
                    s_[:, h * 512:(h + 1) * 512], start=False, stop=True),
                    [tc.x1b, ld_id], tag=f"zadd_{c}_{h}"))
            tc.ffn = ffn[-1]

        if tb is not None:
            qz = V(lambda e: e._custom_dve(
                QZ_OP, out=g.qi[:], in0=g.tq[:], s0=256.0, s1=0.5, imm2=1024.0),
                [tb.tp, ga(b - 1, "qd")], tag=f"qz_{b}")
            tb.qd = V(lambda e: e._custom_dve(
                QN_OP, out=g.qn[:], in0=g.tq[:], in1=g.qi[:], s0=256.0, s1=2.0, imm2=1024.0),
                [qz, ga(b - 1, "q7")], tag=f"qd_{b}")

        if tb is not None:
            par_ = b % 2
            xd_t = g.xd[:, par_ * DM:(par_ + 1) * DM]
            rsl = slice(par_ * 32, (par_ + 1) * 32)
            sbv = g.sbB[:, rsl].unsqueeze(2).broadcast_to([128, 32, 32])
            tb.q7 = G(lambda e, o=xd_t, sbv=sbv: e.tensor_tensor(
                o.rearrange("p (a b) -> p a b", a=32, b=32),
                g.qn[:].rearrange("p (a b) -> p a b", a=32, b=32), sbv, Alu.mult),
                [tb.qd, tb.q2f, ga(b - 2, "evT")], tag=f"q7_{b}")

        if tb is not None:
            par_ = b % 2
            usl = slice(par_ * DM, (par_ + 1) * DM)
            x1b_t = g.x1b[:, usl]
            resid_t = g.resid[:, usl]
            scol = slice(par_, par_ + 1)
            stc = lambda cc: g.st_t1[:, par_ * 6 + cc:par_ * 6 + cc + 1]
            nmr = V(lambda e, sc=scol, c0=stc(0): e.tensor_scalar(
                c0, g.st_m[:, sc], g.st_r[:, sc], -1.0, Alu.mult, Alu.mult),
                [tb.nr4], tag=f"nmr_{b}")
            tb.x1b = S(lambda e, o=x1b_t, s_=resid_t, sc=scol, c0=stc(0): e.activation(
                o, s_, Act.Identity, bias=c0, scale=g.st_r[:, sc]),
                [nmr, ga(b - 2, "ffn")], tag=f"x1b_{b}")
        if ta is not None:
            ta.pr1B = V(lambda e: e.tensor_tensor(g.p1B[:], g.uevB[:], g.ks1[:], Alu.mult),
                        [ta.evB], tag=f"p1B_{a}")
            ta.pr2B = V(lambda e: e.tensor_tensor(g.p2B[:], g.uevB[:], g.ks2[:], Alu.mult),
                        [ta.pr1B], tag=f"p2B_{a}")

        if tc is not None:
            par_ = c % 2
            usl = slice(par_ * DM, (par_ + 1) * DM)
            z_t = g.z[:, usl]
            obuf_t = g.obuf[:, usl]
            scol = slice(par_, par_ + 1)
            stc = lambda cc: g.st_t2[:, par_ * 6 + cc:par_ * 6 + cc + 1]
            tc.zev = S(lambda e, o=z_t, sc=scol: e.activation(
                o, g.psZ[:], Act.Copy, accum_out=g.zsum[:, sc]),
                [tc.ffn, ga(c - 2, "oap")], tag=f"zev_{c}")
            z3 = S(lambda e, s_=z_t, sc=scol: e.activation(
                g.sqjunk[:], s_, Act.Square, accum_out=g.zsq[:, sc]), [tc.zev], tag=f"zsq_{c}")
            w1_ = V(lambda e, sc=scol: e.tensor_scalar(
                g.st2_m[:, sc], g.zsum[:, sc], cW, None, Alu.mult),
                [tc.zev, ga(c - 2, "oap"), ga(c - 2, "w5")], tag=f"w1_{c}")
            w2_ = V(lambda e, sc=scol, c0=stc(0): e.tensor_tensor(
                c0, g.st2_m[:, sc], g.st2_m[:, sc], Alu.mult), [w1_])
            w3_ = V(lambda e, sc=scol, c0=stc(0), c1=stc(1): e.scalar_tensor_tensor(
                c1, g.zsq[:, sc], cW, c0, Alu.mult, Alu.subtract), [w2_, z3], tag=f"w3_{c}")
            w4_ = V(lambda e, c1=stc(1): e.tensor_scalar(c1, c1, EPS, None, Alu.add), [w3_])
            tc.w5 = S(lambda e, c1=stc(1), c2=stc(2): e.activation(c2, c1, Act.Sqrt),
                      [w4_, z3], tag=f"sqrt2_{c}")
            w6_ = V(lambda e, c2=stc(2), c3=stc(3): e.reciprocal(c3, c2), [tc.w5])
            x1_ = V(lambda e, c3=stc(3), c4=stc(4): e.tensor_tensor(c4, c3, c3, Alu.mult), [w6_])
            x2_ = V(lambda e, c1=stc(1), c4=stc(4), c5=stc(5): e.tensor_tensor(c5, c1, c4, Alu.mult), [x1_])
            x3_ = V(lambda e, c5=stc(5): e.tensor_scalar(c5, c5, -0.5, 1.5, Alu.mult, Alu.add), [x2_])
            x4_ = V(lambda e, sc=scol, c3=stc(3), c5=stc(5): e.tensor_tensor(
                g.st2_r[:, sc], c3, c5, Alu.mult), [x3_])
            x5_ = V(lambda e, sc=scol, c0=stc(0): e.tensor_scalar(
                c0, g.st2_m[:, sc], g.st2_r[:, sc], -1.0, Alu.mult, Alu.mult),
                [x4_], tag=f"x5_{c}")
            tc.oap = S(lambda e, o=obuf_t, s_=z_t, sc=scol, c0=stc(0): e.activation(
                o, s_, Act.Identity, bias=c0, scale=g.st2_r[:, sc]),
                [x5_, z3, ga(c - 2, "ost")], tag=f"oap_{c}")
            tc.ost = GD(lambda e, cc=c, s_=obuf_t: e.dma_start(
                out=d_out[cc * 128:(cc + 1) * 128, :], in_=s_), [tc.oap], tag=f"ost_{c}")


    for k in range(NT + 2):
        emit_slot(k)

    final_store = T[NT - 1].ost


    watermark = {}

    def emit_stream(eng_name, ops, extra_final=None):
        def body(eng):
            wm = watermark.setdefault(eng_name, {})
            for o in ops:
                needs = {}
                if o.inc == 16:
                    needs[o.engine] = o.val - 16
                for d in o.deps:
                    if d.engine == o.engine and o.engine == "pe":
                        continue
                    needs[d.engine] = max(needs.get(d.engine, 0), d.val)
                for se, val in sorted(needs.items()):
                    if val <= 0:
                        continue
                    if eng_name != "pe" and wm.get(se, -1) >= val:
                        continue
                    eng.wait_ge(sems[se], val)
                    wm[se] = val
                inst = o.fn(eng)
                inst.then_inc(sems[o.engine], o.inc)
            if extra_final is not None:
                eng.wait_ge(sems[extra_final[0]], extra_final[1])
        return body

    gp_ops = [o for o in P.all if o.engine in ("gd", "gc")]
    s_ops = [o for o in P.all if o.engine in ("s", "sd")]

    with nc.Block() as block:
        block.sync(emit_stream("sync", P.streams.get("sync", [])))
        block.vector(emit_stream("v", P.streams.get("v", [])))
        block.scalar(emit_stream("s", s_ops))
        block.tensor(emit_stream("pe", P.streams.get("pe", [])))
        block.gpsimd(emit_stream("gp", gp_ops,
                                 extra_final=("gd", final_store.val)))

    es.close()
    nc.finalize()
    return nc


_NC_CACHE = {}


def kernel(L0_attn_u, L0_attn_A_real, L0_attn_A_imag, L0_attn_B_real,
           L0_attn_B_imag, L0_attn_C_real, L0_attn_C_imag, L0_attn_D,
           L0_ffn_weight, L0_ffn_bias):
    u = np.ascontiguousarray(np.asarray(L0_attn_u, dtype=np.float32))
    par = np.ascontiguousarray(np.stack(
        [np.asarray(x, np.float32).reshape(-1) for x in
         (L0_attn_A_real, L0_attn_A_imag, L0_attn_B_real, L0_attn_B_imag,
          L0_attn_C_real, L0_attn_C_imag, L0_attn_D, L0_ffn_bias)], axis=0))
    w = np.ascontiguousarray(np.asarray(L0_ffn_weight, np.float32))
    consts = _host_consts()

    if "nc" not in _NC_CACHE:
        _NC_CACHE["nc"] = build_nc()
    nc = _NC_CACHE["nc"]

    in_maps = make_in_maps(u, par, w, consts)
    res = run_bass_kernel_spmd(nc, in_maps, core_ids=list(range(NCORES)))
    out = np.empty((B, L, DM), np.float32)
    for c in range(NCORES):
        b, h = c // 2, c % 2
        out[b, h * TOK:(h + 1) * TOK, :] = np.asarray(res.results[c]["out"])
    return out


def make_in_maps(u, par, w, consts):
    in_maps = []
    for c in range(NCORES):
        b, h = c // 2, c % 2
        usl = np.ascontiguousarray(u[b, h * TOK:(h + 1) * TOK, :])
        halo = np.zeros((128, DM), np.float32)
        if h == 1:
            halo = np.concatenate(
                [np.zeros((STRIDE, DM), np.float32), u[b, TOK - STRIDE:TOK, :]], axis=0)
        halo = np.ascontiguousarray(halo)
        m = {"u": usl, "uhalo": halo, "par": par, "w": w}
        rep = lambda row, n: np.ascontiguousarray(np.broadcast_to(row.reshape(1, DM), (n, DM)))
        m["crr"] = rep(par[4], NF); m["cir"] = rep(par[5], NF)
        m["brr"] = rep(par[2], NF); m["bir"] = rep(par[3], NF)
        m["arr"] = rep(par[0], NF); m["air"] = rep(par[1], NF)
        m["drep"] = rep(par[6], 128)
        m.update(consts)
        in_maps.append(m)
    return in_maps
